# version 14 (final)
# speedup vs baseline: 1.1986x; 1.0392x over previous
import contextlib
import os as _os
_KCUT = int(_os.environ.get("KCUT", "99"))
_CCUT = int(_os.environ.get("CCUT", "99"))

import numpy as np
import ml_dtypes

import concourse.bass as bass
import concourse.mybir as mybir
from concourse.bass_utils import run_bass_kernel_spmd

F32 = mybir.dt.float32
BF16 = mybir.dt.bfloat16
AXX = mybir.AxisListType.X
AF = mybir.ActivationFunctionType
ALU = mybir.AluOpType
BFNP = ml_dtypes.bfloat16

B = 4
T = 4096
C = 2048
NH = 4
D = 128
CKV = 512
I = 5632
TWO_I = 2 * I
NBA = 32
NBO = 16
TOK = 2048

M32 = float(np.float32(1.5 * 2 ** 23))
M52 = float(np.float32(1.5 * 2 ** 52))
NEGM = -30000.0
KB = 1024

ENGINES = ("sync", "vector", "scalar", "tensor", "gpsimd")
SEMS = ("dS", "dC", "dG", "cV", "cC", "cT", "cP")
_OWNED = {"vector": "cV", "scalar": "cC", "tensor": "cT", "gpsimd": "cP"}
_QSEM = {"sync": "dS", "scalar": "dC", "gpsimd": "dG"}


class DmaTok:
    __slots__ = ("queue", "sem", "value", "boundary")

    def __init__(self, queue, sem, value):
        self.queue, self.sem, self.value, self.boundary = queue, sem, value, None


class Prog:

    def __init__(self):
        self.streams = {e: [] for e in ENGINES}
        self.cnt = {s: 0 for s in SEMS}
        self.hwm = {e: {s: 0 for s in SEMS} for e in ENGINES}
        self.committed = {q: 0 for q in _QSEM}
        self._h = None

    def _commit(self, queue):
        sem = _QSEM[queue]
        tot = self.cnt[sem]
        if self.committed[queue] < tot:
            def replay(e, sem=sem, tot=tot):
                e.wait_ge(self._h[sem], tot)
            self.streams[queue].append(replay)
            self.committed[queue] = tot
            if self.hwm[queue][sem] < tot:
                self.hwm[queue][sem] = tot
        return self.committed[queue]

    def _resolve(self, tok):
        if tok is None:
            return None
        if isinstance(tok, DmaTok):
            if tok.boundary is None:
                if self.committed[tok.queue] >= tok.value:
                    tok.boundary = self.committed[tok.queue]
                else:
                    tok.boundary = self._commit(tok.queue)
            return (tok.sem, tok.boundary)
        return tok

    def emit(self, eng, build, waits=(), inc=None, self_wait=False):
        ws = []
        for tok in waits:
            r = self._resolve(tok)
            if r is None:
                continue
            s, v = r
            if _OWNED.get(eng) == s and not self_wait:
                continue
            if self.hwm[eng][s] < v:
                ws.append((s, v))
                self.hwm[eng][s] = v
        tok = None
        if inc is not None:
            amt = 16 if inc.startswith("d") else 1
            self.cnt[inc] += amt
            tok = (inc, self.cnt[inc])

        def replay(e, ws=tuple(ws), build=build, tok=tok):
            for s, v in ws:
                e.wait_ge(self._h[s], v)
            ins = build(e)
            if tok is not None:
                ins.then_inc(self._h[tok[0]], 16 if tok[0].startswith("d") else 1)

        self.streams[eng].append(replay)
        return tok

    def wait_only(self, eng, waits):
        ws = []
        for tok in waits:
            r = self._resolve(tok)
            if r is None:
                continue
            s, v = r
            if _OWNED.get(eng) == s:
                continue
            if self.hwm[eng][s] < v:
                ws.append((s, v))
                self.hwm[eng][s] = v
        if ws:
            def replay(e, ws=tuple(ws)):
                for s, v in ws:
                    e.wait_ge(self._h[s], v)
            self.streams[eng].append(replay)

    def now(self, sem):
        if sem.startswith("d"):
            q = {v: k for k, v in _QSEM.items()}[sem]
            return DmaTok(q, sem, self.cnt[sem])
        return (sem, self.cnt[sem])

    def barrier(self):
        snap = [self.now(s) for s in SEMS if self.cnt[s] > 0]
        for e in ENGINES:
            self.wait_only(e, snap)



def _rope_tables(positions):
    nb = len(positions) // 128
    import os
    use_jax = os.environ.get("JAX_PLATFORMS", "").strip().lower() == "cpu"
    try:
        if not use_jax:
            raise RuntimeError("numpy tables")
        import jax
        with jax.default_device(jax.devices("cpu")[0]):
            import jax.numpy as jnp
            invf = 1.0 / 10000.0 ** (jnp.arange(D // 2, dtype=jnp.float32) * 2.0 / D)
            emb = jnp.asarray(np.asarray(positions, np.float32))[:, None] * invf[None, :]
            cos = np.asarray(jnp.cos(emb), dtype=np.float32)
            sin = np.asarray(jnp.sin(emb), dtype=np.float32)
    except Exception:
        invf = (1.0 / np.power(np.float32(10000.0),
                               np.arange(D // 2, dtype=np.float32)
                               * np.float32(2.0) / np.float32(D))).astype(np.float32)
        emb = np.asarray(positions, np.float32)[:, None] * invf[None, :]
        cos = np.cos(emb).astype(np.float32)
        sin = np.sin(emb).astype(np.float32)
    cos = cos.reshape(nb, 128, 64).transpose(1, 0, 2).reshape(128, nb * 64)
    sin = sin.reshape(nb, 128, 64).transpose(1, 0, 2).reshape(128, nb * 64)
    return np.ascontiguousarray(cos), np.ascontiguousarray(sin)


def _host_inputs_for_core(core, x, weights):
    b, p = core // 2, core % 2
    xb = np.ascontiguousarray(x[b])
    xo = np.ascontiguousarray(xb.reshape(NBA, 128, C)[p::2].reshape(TOK, C))
    pos_own = ((np.arange(NBO)[:, None] * 2 + p) * 128
               + np.arange(128)[None, :]).reshape(-1)
    cq, sq = _rope_tables(pos_own)
    ck, sk = _rope_tables(np.arange(T))
    inv_sqrt_d = np.float32(1.0) / np.sqrt(np.float32(D))
    cq = (cq * inv_sqrt_d).astype(np.float32)
    sq = (sq * inv_sqrt_d).astype(np.float32)
    tri = np.triu(np.full((128, 128), NEGM, np.float32), 1)
    if p == 1:
        mask = np.concatenate([np.zeros((128, 128), np.float32), tri], axis=1)
    else:
        mask = np.concatenate([tri, np.full((128, 128), NEGM, np.float32)], axis=1)
    cst = np.zeros((1, 16), np.float32)
    cst[0, 0] = 0.75
    cst[0, 1] = M52
    cst[0, 2] = M32
    cst[0, 3] = 1e-5
    cst[0, 4] = 1.0 / C
    cst[0, 5] = 1.0 / 127.0
    cst[0, 6] = 1e-8
    d = {"x_seq": xb, "x_own": xo, "cs_q": cq, "sn_q": sq, "cs_k": ck,
         "sn_k": sk, "mask": mask, "cst": cst,
         "ident": np.eye(128, dtype=BFNP)}
    d.update(weights)
    return d



PHASES = ("p0a", "p0b", "attn", "gate", "down")


def build_nc(debug=(), phases=PHASES):
    nc = bass.Bass("TRN2", target_bir_lowering=False,
                   detect_race_conditions=False)
    es = contextlib.ExitStack()

    def din(name, shape, dt=F32):
        return nc.dram_tensor(name, shape, dt, kind="ExternalInput").ap()

    x_seq = din("x_seq", [T, C])
    x_own = din("x_own", [TOK, C])
    wq_d = din("wq", [C, C])
    wk_d = din("wk", [C, CKV])
    wv_d = din("wv", [C, CKV])
    wo_d = din("wo", [CKV, C])
    bq_d = din("bq", [1, C])
    bk_d = din("bk", [1, CKV])
    bv_d = din("bv", [1, CKV])
    bo_d = din("bo", [1, C])
    w1_d = din("w1", [TWO_I, C])
    w2_d = din("w2", [C, I])
    csq_d = din("cs_q", [128, NBO * 64])
    snq_d = din("sn_q", [128, NBO * 64])
    csk_d = din("cs_k", [128, NBA * 64])
    snk_d = din("sn_k", [128, NBA * 64])
    mask_d = din("mask", [128, 256])
    cst_d = din("cst", [1, 16])
    ident_d = din("ident", [128, 128], BF16)

    out_d = nc.dram_tensor("out", [TOK, C], F32, kind="ExternalOutput").ap()

    xn_s = nc.dram_tensor("xn_s", [128, NBO * 2048], BF16, kind="Internal").ap()
    qx_s = nc.dram_tensor("qx_s", [128, NBO * 2048], BF16, kind="Internal").ap()
    t_sd = nc.dram_tensor("t_s", [TOK, I], BF16, kind="Internal").ap()
    hq_s = nc.dram_tensor("hq_s", [128, NBO * I], BF16, kind="Internal").ap()
    ax_s = nc.dram_tensor("ax_s", [TOK, C], F32, kind="Internal").ap()

    dbg_shapes = {"xn_s": ([128, NBO * 2048], BF16), "qx_s": ([128, NBO * 2048], BF16),
                  "kf": ([128, NH * T], BF16), "vt": ([128, NBA * CKV], BF16),
                  "qf": ([128, NH * TOK], BF16), "ax_s": ([TOK, C], F32),
                  "t_s": ([TOK, I], BF16), "hq_s": ([128, NBO * I], BF16),
                  "sx": ([128, 16], F32), "sh": ([128, 16], F32),
                  "stt": ([128, 432], F32)}
    dbg = {}
    for name in debug:
        sh, dt = dbg_shapes[name]
        dbg[name] = nc.dram_tensor("dbg_" + name, sh, dt,
                                   kind="ExternalOutput").ap()

    SB_BASE = 17408

    def sb(name, shape, dt, off):
        return nc.alloc_sbuf_tensor_at(name, shape, dt, offset=SB_BASE + off)

    kf = sb("kf", [128, NH * T], BF16, 0)
    vt = sb("vt", [128, NBA * CKV], BF16, 32 * KB)
    qf = sb("qf", [128, NH * TOK], BF16, 64 * KB)
    AM = 80 * KB
    cstb = sb("cstb", [128, 16], F32, AM)
    meff = sb("meff", [128, 1], F32, AM + 64)
    scol = sb("scol", [128, 1], F32, AM + 96)
    ccol = sb("ccol", [128, 16], F32, AM + 128)
    ccolS = sb("ccolS", [128, 4], F32, AM + 7872)
    sxc = sb("sxc", [128, 16], F32, AM + 192)
    rmx = sb("rmx", [128, 16], F32, AM + 256)
    shcol = sb("shcol", [128, 16], F32, AM + 320)
    rscol = sb("rscol", [128, 16], F32, AM + 384)
    identsb = sb("identsb", [128, 128], BF16, AM + 448)
    masksb = sb("masksb", [128, 256], F32, AM + 704)
    bqe = sb("bqe", [128, CKV], F32, AM + 1728)
    bkr = sb("bkr", [128, CKV], F32, AM + 3776)
    bvr = sb("bvr", [128, CKV], F32, AM + 5824)
    csq = sb("csq", [128, NBO * 64], F32, 88 * KB)
    snq = sb("snq", [128, NBO * 64], F32, 92 * KB)
    B0 = 96 * KB

    psF = [nc.alloc_psum_tensor(f"psf{i}", [128, 512], F32) for i in range(4)]
    psB = [nc.alloc_psum_tensor(f"psb{i}", [128, 1024], BF16) for i in range(4)]

    sems = {s: es.enter_context(nc.semaphore(s)) for s in SEMS}
    P = Prog()
    P._h = sems

    def dma(queue, out_ap, in_ap, waits=()):
        P.emit(queue, lambda e, o=out_ap, i=in_ap: e.dma_start(out=o, in_=i),
               waits=waits, inc=_QSEM[queue])
        return DmaTok(queue, _QSEM[queue], P.cnt[_QSEM[queue]])

    def transpose_group(src_tile, tiles, dst_tile, dst_offs, banks, bank_free,
                        waits, evac_eng="scalar"):
        evacs = []
        n = len(tiles)
        for bi in range((n + 7) // 8):
            pb = banks[bi % len(banks)]
            bid = id(pb)
            k0, k1 = bi * 8, min(bi * 8 + 8, n)
            w = list(waits) + ([bank_free[bid]] if bank_free.get(bid) else [])
            t_tr = None
            for k in range(k0, k1):
                t_tr = P.emit(
                    "tensor",
                    lambda e, sk=tiles[k], pb=pb, sl=(k - k0): e.transpose(
                        out=pb[:, sl * 128:(sl + 1) * 128],
                        in_=src_tile[:, sk * 128:(sk + 1) * 128],
                        identity=identsb[:]),
                    waits=w if k == k0 else (),
                    inc="cT" if k == k1 - 1 else None)
            nb_ = k1 - k0
            offs = dst_offs[k0:k1]
            stride = offs[1] - offs[0] if nb_ > 1 else 128
            uniform = all(offs[kk] - offs[0] == kk * stride for kk in range(nb_))
            W = int(dst_tile.shape[1])
            if uniform and W % stride == 0 and offs[0] % stride + 128 <= stride:
                q0, rc = divmod(offs[0], stride)
                dst_ap = dst_tile[:].rearrange("p (n r) -> p n r", r=stride)[
                    :, q0:q0 + nb_, rc:rc + 128]
                src_ap = pb[:, 0:nb_ * 128].rearrange("p (n r) -> p n r", r=128)
                if evac_eng == "scalar":
                    et = P.emit("scalar", lambda e, d=dst_ap, s=src_ap: e.copy(
                        out=d, in_=s), waits=[t_tr], inc="cC")
                else:
                    et = P.emit("vector", lambda e, d=dst_ap, s=src_ap:
                                e.tensor_copy(out=d, in_=s),
                                waits=[t_tr], inc="cV")
            else:
                et = None
                for k in range(k0, k1):
                    is_last = k == k1 - 1
                    eng2 = evac_eng
                    fn = (lambda e, do=dst_offs[k], pb=pb, sl=(k - k0): e.copy(
                        out=dst_tile[:, do:do + 128],
                        in_=pb[:, sl * 128:(sl + 1) * 128])) if eng2 == "scalar"                         else (lambda e, do=dst_offs[k], pb=pb, sl=(k - k0):
                              e.tensor_copy(
                        out=dst_tile[:, do:do + 128],
                        in_=pb[:, sl * 128:(sl + 1) * 128]))
                    et = P.emit(eng2, fn, waits=[t_tr] if k == k0 else (),
                                inc=("cC" if eng2 == "scalar" else "cV")
                                if is_last else None)
            bank_free[bid] = et
            evacs.append(et)
        return evacs

    def rope_emit(stage_f32, out_bf, tbl_c, tbl_s, blk64, first_waits, tiles):
        t_last = None
        t0, t1, t2, t3 = tiles
        for h in range(NH):
            b0h = h * 128
            x1 = stage_f32[:, b0h + 0:b0h + 128:2]
            x2 = stage_f32[:, b0h + 1:b0h + 128:2]
            cc = tbl_c[:, blk64 * 64:(blk64 + 1) * 64]
            ss = tbl_s[:, blk64 * 64:(blk64 + 1) * 64]
            P.emit("vector", lambda e, x1=x1, cc=cc, t=t0: e.tensor_mul(
                out=t[:], in0=x1, in1=cc),
                waits=first_waits if h == 0 else ())
            P.emit("vector", lambda e, x2=x2, ss=ss, t=t1: e.tensor_mul(
                out=t[:], in0=x2, in1=ss))
            P.emit("vector", lambda e, x1=x1, ss=ss, t=t2: e.tensor_mul(
                out=t[:], in0=x1, in1=ss))
            tp = P.emit("vector", lambda e, x2=x2, cc=cc, t=t3: e.tensor_mul(
                out=t[:], in0=x2, in1=cc), inc="cV")
            P.emit("vector", lambda e, b0h=b0h, o=out_bf, ta=t0, tb=t1: e.tensor_sub(
                out=o[:, b0h:b0h + 64], in0=ta[:], in1=tb[:]),
                waits=[tp], self_wait=True)
            t_last = P.emit("vector", lambda e, b0h=b0h, o=out_bf, ta=t2, tb=t3: e.tensor_add(
                out=o[:, b0h + 64:b0h + 128], in0=ta[:], in1=tb[:]),
                inc="cV" if h == NH - 1 else None)
        return t_last

    t_cst = dma("scalar", cstb[:], cst_d[0:1, :].partition_broadcast(128))
    dma("scalar", identsb[:], ident_d)
    t_id = P.now("dC")
    t_mask = dma("scalar", masksb[:], mask_d)
    t_csq = dma("scalar", csq[:], csq_d)
    t_snq = dma("scalar", snq[:], snq_d)
    t_bk = dma("scalar", bkr[:], bk_d[0:1, :].partition_broadcast(128))
    t_bv = dma("scalar", bvr[:], bv_d[0:1, :].partition_broadcast(128))

    P.emit("vector", lambda e: e.memset(shcol[:], 0.0))
    t0 = P.emit("vector", lambda e: e.tensor_single_scalar(
        out=scol[:], in_=cstb[:, 0:1], scalar=cstb[:, 1:2], op=ALU.add),
        waits=[t_cst], inc="cV")
    t0 = P.emit("vector", lambda e: e.tensor_single_scalar(
        out=scol[:], in_=scol[:], scalar=cstb[:, 1:2], op=ALU.subtract),
        waits=[t0], self_wait=True, inc="cV")
    t0 = P.emit("vector", lambda e: e.tensor_single_scalar(
        out=meff[:], in_=scol[:], scalar=cstb[:, 1:2], op=ALU.mult),
        waits=[t0], self_wait=True, inc="cV")
    t_meff = P.emit("vector", lambda e: e.tensor_single_scalar(
        out=meff[:], in_=meff[:], scalar=cstb[:, 2:3], op=ALU.add),
        waits=[t0], self_wait=True, inc="cV")

    bqr_tmp = sb("bqr_tmp", [128, C], F32, B0)
    t_bq = dma("scalar", bqr_tmp[:], bq_d[0:1, :].partition_broadcast(128))
    bqr_r = bqr_tmp[:].rearrange("p (h g j) -> p g h j", h=NH, g=4, j=D)
    bqe_r = bqe[:].rearrange("p (h j) -> p h j", h=NH, j=D)
    P.emit("vector", lambda e: e.tensor_add(
        out=bqe_r, in0=bqr_r[:, 0], in1=bqr_r[:, 1]), waits=[t_bq])
    P.emit("vector", lambda e: e.tensor_add(
        out=bqe_r, in0=bqe_r, in1=bqr_r[:, 2]))
    t_bqe = P.emit("vector", lambda e: e.tensor_add(
        out=bqe_r, in0=bqe_r, in1=bqr_r[:, 3]), inc="cV")

    if "p0a" in phases:
        csk = sb("csk", [128, NBA * 64], F32, B0 + 8 * KB)
        snk = sb("snk", [128, NBA * 64], F32, B0 + 16 * KB)
        xbuf = [sb(f"xb{i}", [128, C], F32, B0 + 24 * KB + i * 8 * KB)
                for i in range(2)]
        xbuf4 = xbuf + [sb("xb2", [128, C], F32, B0 + 48 * KB)]
        scrap = sb("scrap", [128, C], BF16, B0 + 40 * KB)
        xnt = sb("xnt", [128, C], BF16, B0 + 48 * KB)
        qxt = sb("qxt", [128, C], BF16, B0 + 52 * KB)
        xnfb = sb("xnfb", [128, 16 * 128], BF16, B0 + 56 * KB)
        kstg = sb("kstg", [128, CKV], F32, B0 + 60 * KB)
        ktb = sb("ktb", [128, CKV], BF16, B0 + 62 * KB)
        rope_t = [sb(f"rt{i}", [128, 64], F32, B0 + 63 * KB + i * 256)
                  for i in range(16)]
        wkv32 = [sb(f"wkv32_{i}", [128, 2 * CKV], F32, B0 + 67 * KB + i * 4 * KB)
                 for i in range(2)]
        wksb = sb("wksb", [128, 16 * CKV], BF16, B0 + 75 * KB)
        wvsb = sb("wvsb", [128, 16 * CKV], BF16, B0 + 91 * KB)
        ST = B0 + 107 * KB
        ssum_a = sb("ssum_a", [128, NBA], F32, ST)
        ssq_a = sb("ssq_a", [128, NBA], F32, ST + 128)
        m_a = sb("m_a", [128, NBA], F32, ST + 256)
        va_a = sb("va_a", [128, NBA], F32, ST + 384)
        std_a = sb("std_a", [128, NBA], F32, ST + 512)
        inv_a = sb("inv_a", [128, NBA], F32, ST + 640)
        ngm_a = sb("ngm_a", [128, NBA], F32, ST + 768)
        ssum_o = sb("ssum_o", [128, NBO], F32, ST + 896)
        ssq_o = sb("ssq_o", [128, NBO], F32, ST + 960)
        m_o = sb("m_o", [128, NBO], F32, ST + 1024)
        va_o = sb("va_o", [128, NBO], F32, ST + 1088)
        std_o = sb("std_o", [128, NBO], F32, ST + 1152)
        inv_o = sb("inv_o", [128, NBO], F32, ST + 1216)
        ngm_o = sb("ngm_o", [128, NBO], F32, ST + 1280)
        rmax_o = sb("rmax_o", [128, NBO], F32, ST + 1344)
        rmn_o = sb("rmn_o", [128, NBO], F32, ST + 1408)
        isx_o = sb("isx_o", [128, NBO], F32, ST + 1472)
        sc2_o = sb("sc2_o", [128, NBO], F32, ST + 1536)
        b2_o = sb("b2_o", [128, NBO], F32, ST + 1600)

        t_csk = dma("scalar", csk[:], csk_d)
        t_snk = dma("scalar", snk[:], snk_d)
        if "stt" in dbg:
            stall = sb("stall", [128, 432], F32, ST)

        wkv_done = [None, None]

        def emit_wkv(c):
            buf = wkv32[c % 2]
            w = [wkv_done[c % 2]] if wkv_done[c % 2] else []
            tl1 = dma("sync", buf[:, 0:CKV], wk_d[c * 128:(c + 1) * 128, :],
                      waits=w)
            tl2 = dma("sync", buf[:, CKV:2 * CKV],
                      wv_d[c * 128:(c + 1) * 128, :])
            P.emit("vector", lambda e, c=c, b=buf: e.tensor_copy(
                out=wksb[:, c * CKV:(c + 1) * CKV], in_=b[:, 0:CKV]),
                waits=[tl1, tl2])
            wkv_done[c % 2] = P.emit(
                "vector", lambda e, c=c, b=buf: e.tensor_copy(
                    out=wvsb[:, c * CKV:(c + 1) * CKV], in_=b[:, CKV:2 * CKV]),
                inc="cV")

        xo4 = [sb("xo4a", [128, C], F32, 0),
               sb("xo4b", [128, C], F32, 8 * KB)]
        scrap2 = sb("scrap2", [128, C], F32, 16 * KB)
        xnfbB = sb("xnfbB", [128, 16 * 128], BF16, 56 * KB)
        scrapB = sb("scrapB", [128, C], BF16, 24 * KB)
        sq_state = {"n": 0, "rd": [None, None]}

        def emit_stats(buf, jj, tx, own):
            sm = ssum_o[:, jj:jj + 1] if own else ssum_a[:, jj:jj + 1]
            sq = ssq_o[:, jj:jj + 1] if own else ssq_a[:, jj:jj + 1]
            n = sq_state["n"]
            sq_state["n"] += 1
            scr = scrap if n % 2 == 0 else scrapB
            w = [tx] + ([sq_state["rd"][n % 2]] if sq_state["rd"][n % 2] else [])
            t_sq = P.emit("scalar", lambda e, b=buf, scr=scr: e.activation(
                out=scr[:], in_=b[:], func=AF.Square), waits=w, inc="cC")
            P.emit("vector", lambda e, b=buf, sm=sm: e.reduce_sum(
                out=sm, in_=b[:], axis=AXX), waits=[tx])
            t_v = P.emit("vector", lambda e, scr=scr, sq=sq: e.reduce_sum(
                out=sq, in_=scr[:], axis=AXX), waits=[t_sq],
                inc="cV")
            sq_state["rd"][n % 2] = t_v
            if own:
                P.emit("vector", lambda e, b=buf, jj=jj: e.tensor_reduce(
                    out=rmax_o[:, jj:jj + 1], in_=b[:], axis=AXX, op=ALU.max))
                t_v = P.emit("vector", lambda e, b=buf, jj=jj: e.tensor_reduce(
                    out=rmn_o[:, jj:jj + 1], in_=b[:], axis=AXX, op=ALU.min,
                    negate=True), inc="cV")
            return [t_sq, t_v]

        xb_free4 = [None, None, None]
        for jj in range(NBO):
            buf = xbuf4[jj % 3]
            w = xb_free4[jj % 3] or []
            tx = dma("scalar" if jj % 2 == 0 else "gpsimd",
                     buf[:], x_own[jj * 128:(jj + 1) * 128, :], waits=w)
            xb_free4[jj % 3] = emit_stats(buf, jj, tx, own=True)

        tVo = P.now("cV")

        _cx = [0]

        def wide(fn, waits=(), inc="cV", sw=True):
            _cx[0] += 1
            if _cx[0] > _CCUT:
                return None
            return P.emit("vector", fn, waits=[t for t in waits if t],
                          self_wait=sw, inc=inc)

        def dmy():
            wide(lambda e: e.tensor_copy(out=scol[:], in_=cstb[:, 0:1]))

        wide(lambda e: e.tensor_single_scalar(
            out=m_o[:], in_=ssum_o[:], scalar=cstb[:, 4:5], op=ALU.mult),
            waits=[tVo])
        wide(lambda e: e.tensor_single_scalar(
            out=va_o[:], in_=ssq_o[:], scalar=cstb[:, 4:5], op=ALU.mult))
        wide(lambda e: e.tensor_sub(out=rmax_o[:], in0=rmax_o[:], in1=m_o[:]))
        wide(lambda e: e.tensor_add(out=rmn_o[:], in0=rmn_o[:], in1=m_o[:]))
        wide(lambda e: e.tensor_mul(out=ngm_o[:], in0=m_o[:], in1=m_o[:]))
        wide(lambda e: e.tensor_max(out=rmax_o[:], in0=rmax_o[:], in1=rmn_o[:]))
        t1 = wide(lambda e: e.tensor_sub(out=va_o[:], in0=va_o[:], in1=ngm_o[:]))
        dmy()
        tsb = P.emit("scalar", lambda e: e.activation(
            out=std_o[:], in_=va_o[:], func=AF.Sqrt, bias=cstb[:, 3:4],
            scale=1.0), waits=[t1], inc="cC") if _CCUT > 6 else None
        wide(lambda e: e.reciprocal(out=inv_o[:], in_=std_o[:]), waits=[tsb])
        dmy()
        wide(lambda e: e.tensor_mul(out=rmax_o[:], in0=rmax_o[:], in1=inv_o[:]))
        wide(lambda e: e.tensor_mul(out=ngm_o[:], in0=m_o[:], in1=inv_o[:]))
        wide(lambda e: e.tensor_single_scalar(
            out=rmax_o[:], in_=rmax_o[:], scalar=cstb[:, 5:6], op=ALU.mult))
        wide(lambda e: e.tensor_single_scalar(
            out=ngm_o[:], in_=ngm_o[:], scalar=-1.0, op=ALU.mult))
        wide(lambda e: e.tensor_single_scalar(
            out=sxc[:], in_=rmax_o[:], scalar=cstb[:, 6:7], op=ALU.max))
        dmy()
        wide(lambda e: e.reciprocal(out=isx_o[:], in_=sxc[:]))
        dmy()
        wide(lambda e: e.tensor_mul(out=sc2_o[:], in0=inv_o[:], in1=isx_o[:]))
        t_cols_o = wide(lambda e: e.tensor_mul(
            out=b2_o[:], in0=ngm_o[:], in1=isx_o[:]))

        xb2_rd = xb_free4[2]
        state = {"bank": {}, "psK": None, "psV": None, "psR": None,
                 "xnt_rd": None, "qxt_rd": None,
                 "xnA_dma": None, "xnB_dma": None, "scrap_rd": None}
        xbufS = xbuf + [sb("xbq0", [128, C], F32, 64 * KB),
                        sb("xbq1", [128, C], F32, 72 * KB)]
        xb_freeS = [xb_free4[0], xb_free4[1], None, None]
        xo_free = [None, None]
        for jj in range(NBO if _KCUT >= 2 else 0):
            emit_wkv(jj)
            for blk in (2 * jj, 2 * jj + 1):
                buf = xbufS[blk % 4]
                w = xb_freeS[blk % 4] or []
                tx = dma("scalar" if blk % 2 == 0 else "sync",
                         buf[:], x_seq[blk * 128:(blk + 1) * 128, :],
                         waits=w)
                xb_freeS[blk % 4] = emit_stats(buf, blk, tx, own=False)
            if _KCUT < 3:
                continue
            buf = xo4[jj % 2]
            w = [xo_free[jj % 2]] if xo_free[jj % 2] else []
            tx = dma("gpsimd", buf[:], x_own[jj * 128:(jj + 1) * 128, :],
                     waits=w)
            t_xn = P.emit("scalar", lambda e, b=buf, jj=jj: e.activation(
                out=xnt[:], in_=b[:], func=AF.Identity,
                bias=ngm_o[:, jj:jj + 1], scale=inv_o[:, jj:jj + 1]),
                waits=[tx, t_cols_o, state["xnt_rd"]]
                + (xb2_rd if jj == 0 else []), inc="cC")
            t_uq = P.emit("scalar", lambda e, b=buf, jj=jj: e.activation(
                out=scrap2[:], in_=b[:], func=AF.Identity,
                bias=b2_o[:, jj:jj + 1], scale=sc2_o[:, jj:jj + 1]),
                waits=[state["scrap_rd"]], inc="cC")
            xo_free[jj % 2] = t_uq
            P.emit("vector", lambda e: e.tensor_single_scalar(
                out=scrap2[:], in_=scrap2[:], scalar=meff[:, 0:1], op=ALU.add),
                waits=[t_uq, t_meff])
            t_qx = P.emit("vector", lambda e: e.tensor_single_scalar(
                out=qxt[:], in_=scrap2[:], scalar=meff[:, 0:1],
                op=ALU.subtract), waits=[state["qxt_rd"]]
                + (xb2_rd if jj == 0 else []), inc="cV")
            state["scrap_rd"] = t_qx
            evacs = transpose_group(xnt, list(range(16)), xnfb,
                                    [k * 128 for k in range(16)],
                                    [psB[0], psB[1]], state["bank"],
                                    [t_xn, state["xnA_dma"], t_id],
                                    evac_eng="scalar")
            state["xnt_rd"] = evacs[-1]
            state["xnA_dma"] = dma(
                "gpsimd", xn_s[:, jj * 2048:(jj + 1) * 2048], xnfb[:],
                waits=evacs)
            evacs = transpose_group(qxt, list(range(16)), xnfbB,
                                    [k * 128 for k in range(16)],
                                    [psB[0], psB[1]], state["bank"],
                                    [t_qx, state["xnB_dma"], t_id],
                                    evac_eng="vector")
            state["qxt_rd"] = evacs[-1]
            state["xnB_dma"] = dma(
                "gpsimd", qx_s[:, jj * 2048:(jj + 1) * 2048], xnfbB[:],
                waits=evacs)

        tVs = P.now("cV")
        wide(lambda e: e.tensor_single_scalar(
            out=m_a[:], in_=ssum_a[:], scalar=cstb[:, 4:5], op=ALU.mult),
            waits=[tVs])
        wide(lambda e: e.tensor_single_scalar(
            out=va_a[:], in_=ssq_a[:], scalar=cstb[:, 4:5], op=ALU.mult))
        wide(lambda e: e.tensor_mul(out=ngm_a[:], in0=m_a[:], in1=m_a[:]))
        dmy()
        t1 = wide(lambda e: e.tensor_sub(out=va_a[:], in0=va_a[:],
                                         in1=ngm_a[:]))
        dmy()
        tsa = P.emit("scalar", lambda e: e.activation(
            out=std_a[:], in_=va_a[:], func=AF.Sqrt, bias=cstb[:, 3:4],
            scale=1.0), waits=[t1], inc="cC")
        wide(lambda e: e.reciprocal(out=inv_a[:], in_=std_a[:]), waits=[tsa])
        dmy()
        wide(lambda e: e.tensor_mul(out=ngm_a[:], in0=m_a[:], in1=inv_a[:]))
        dmy()
        t_cols = wide(lambda e: e.tensor_single_scalar(
            out=ngm_a[:], in_=ngm_a[:], scalar=-1.0, op=ALU.mult))

        t_wkv = wkv_done[1]
        xb_free = xb_freeS
        for blk in range(NBA if _KCUT >= 4 else 0):
            buf = xbufS[blk % 4]
            w = xb_free[blk % 4] or []
            tx = dma(("scalar", "sync", "gpsimd")[blk % 3],
                     buf[:], x_seq[blk * 128:(blk + 1) * 128, :],
                     waits=w)
            t_xn = P.emit("scalar", lambda e, b=buf, blk=blk: e.activation(
                out=xnt[:], in_=b[:], func=AF.Identity,
                bias=ngm_a[:, blk:blk + 1], scale=inv_a[:, blk:blk + 1]),
                waits=[tx, t_cols, state["xnt_rd"]], inc="cC")
            xb_free[blk % 4] = [t_xn]
            evacs = transpose_group(xnt, list(range(16)), xnfb,
                                    [k * 128 for k in range(16)],
                                    [psB[0], psB[1]], state["bank"],
                                    [t_xn, t_id, state["xnA_dma"]],
                                    evac_eng="scalar")
            t_xnf = evacs[-1]
            state["xnA_dma"] = None
            state["xnt_rd"] = t_xnf
            w0 = [t_xnf, t_wkv] + ([state["psK"]] if state["psK"] else [])
            t_k = None
            for c in range(16):
                t_k = P.emit("tensor", lambda e, c=c: e.matmul(
                    out=psF[0][:], lhsT=xnfb[:, c * 128:(c + 1) * 128],
                    rhs=wksb[:, c * CKV:(c + 1) * CKV],
                    start=(c == 0), stop=(c == 15)),
                    waits=w0 if c == 0 else (), inc="cT" if c == 15 else None)
            w1w = [state["psV"]] if state["psV"] else []
            t_v = None
            for c in range(16):
                t_v = P.emit("tensor", lambda e, c=c: e.matmul(
                    out=psF[1][:], lhsT=xnfb[:, c * 128:(c + 1) * 128],
                    rhs=wvsb[:, c * CKV:(c + 1) * CKV],
                    start=(c == 0), stop=(c == 15)),
                    waits=w1w if c == 0 else (), inc="cT" if c == 15 else None)
            t_kb = P.emit("vector", lambda e: e.tensor_add(
                out=kstg[:], in0=psF[0][:], in1=bkr[:]),
                waits=[t_k, t_bk], inc="cV")
            state["psK"] = t_kb
            tp = None
            for h in range(NH):
                b0h = h * 128
                x1 = kstg[:, b0h + 0:b0h + 128:2]
                x2 = kstg[:, b0h + 1:b0h + 128:2]
                cc = csk[:, blk * 64:(blk + 1) * 64]
                ss = snk[:, blk * 64:(blk + 1) * 64]
                P.emit("vector", lambda e, x1=x1, cc=cc, t=rope_t[4 * h]:
                       e.tensor_mul(out=t[:], in0=x1, in1=cc),
                       waits=[t_kb, t_csk, t_snk] if h == 0 else ())
                P.emit("vector", lambda e, x2=x2, ss=ss, t=rope_t[4 * h + 1]:
                       e.tensor_mul(out=t[:], in0=x2, in1=ss))
                P.emit("vector", lambda e, x1=x1, ss=ss, t=rope_t[4 * h + 2]:
                       e.tensor_mul(out=t[:], in0=x1, in1=ss))
                tp = P.emit("vector", lambda e, x2=x2, cc=cc, t=rope_t[4 * h + 3]:
                            e.tensor_mul(out=t[:], in0=x2, in1=cc),
                            inc="cV" if h == NH - 1 else None)
            t_rope = None
            for h in range(NH):
                b0h = h * 128
                P.emit("vector", lambda e, b0h=b0h, ta=rope_t[4 * h],
                       tb=rope_t[4 * h + 1]: e.tensor_sub(
                    out=ktb[:, b0h:b0h + 64], in0=ta[:], in1=tb[:]),
                    waits=[tp] if h == 0 else (), self_wait=True)
                t_rope = P.emit("vector", lambda e, b0h=b0h, ta=rope_t[4 * h + 2],
                                tb=rope_t[4 * h + 3]: e.tensor_add(
                    out=ktb[:, b0h + 64:b0h + 128], in0=ta[:], in1=tb[:]),
                    inc="cV" if h == NH - 1 else None)
            wR = [t_rope, t_id] + ([state["psR"]] if state["psR"] else [])
            t_tr = None
            for h in range(NH):
                t_tr = P.emit("tensor", lambda e, h=h: e.transpose(
                    out=psB[2][:, h * 128:(h + 1) * 128],
                    in_=ktb[:, h * 128:(h + 1) * 128], identity=identsb[:]),
                    waits=wR if h == 0 else (), inc="cT" if h == NH - 1 else None)
            kdst = kf[:].rearrange("p (n r) -> p n r", r=T)[
                :, :, blk * 128:(blk + 1) * 128]
            ksrc = psB[2][:, 0:NH * 128].rearrange("p (n r) -> p n r", r=128)
            t_ke = P.emit("scalar", lambda e, d=kdst, s=ksrc: e.copy(
                out=d, in_=s), waits=[t_tr], inc="cC")
            state["psR"] = t_ke
            t_ve = P.emit("vector", lambda e, blk=blk: e.tensor_add(
                out=vt[:, blk * CKV:(blk + 1) * CKV], in0=psF[1][:], in1=bvr[:]),
                waits=[t_v, t_bv]
                + ([state["xnB_dma"]] if blk == 0 else []), inc="cV")
            if blk == 0:
                state["xnB_dma"] = None
            state["psV"] = t_ve

    if "p0b" in phases:
        P.barrier()
        wq32 = [sb(f"wq32_{i}", [128, C], F32, B0 + i * 8 * KB) for i in range(2)]
        wqsb = sb("wqsb", [128, 16 * C], BF16, B0 + 16 * KB)
        xnq = sb("xnq", [128, 16 * 128], BF16, B0 + 80 * KB)
        qstg = sb("qstg", [128, CKV], F32, B0 + 84 * KB)
        qtb = sb("qtb", [128, CKV], BF16, B0 + 86 * KB)
        tmq = [sb(f"tmq{i}", [128, 64], F32, B0 + 87 * KB + i * 256)
               for i in range(4)]
        rope_t = tmq

        wq_done = [None, None]
        for c in range(16):
            buf = wq32[c % 2]
            w = [wq_done[c % 2]] if wq_done[c % 2] else []
            tl = dma("sync" if c % 2 == 0 else "scalar",
                     buf[:], wq_d[c * 128:(c + 1) * 128, :], waits=w)
            wq_done[c % 2] = P.emit("vector", lambda e, c=c, b=buf: e.tensor_copy(
                out=wqsb[:, c * C:(c + 1) * C], in_=b[:]), waits=[tl], inc="cV")
        t_wq = wq_done[1]

        psQ_free = None
        psR_free = None
        xn_rd = None
        for j in range(NBO):
            w = [xn_rd] if xn_rd else []
            tl = dma("gpsimd", xnq[:], xn_s[:, j * 2048:(j + 1) * 2048],
                     waits=w)
            wq0 = [tl, t_wq] + ([psQ_free] if psQ_free else [])
            t_q = None
            n = 0
            for g in range(4):
                for c in range(16):
                    rhs = wqsb[:, c * C:(c + 1) * C].rearrange(
                        "p (h g j) -> p g h j", h=NH, g=4, j=D)[:, g]
                    last = (g == 3 and c == 15)
                    t_q = P.emit("tensor", lambda e, c=c, rhs=rhs, n=n: e.matmul(
                        out=psF[0][:], lhsT=xnq[:, c * 128:(c + 1) * 128],
                        rhs=rhs, start=(n == 0), stop=(n == 63)),
                        waits=wq0 if n == 0 else (), inc="cT" if last else None)
                    n += 1
            t_qb = P.emit("vector", lambda e: e.tensor_add(
                out=qstg[:], in0=psF[0][:], in1=bqe[:]),
                waits=[t_q, t_bqe], inc="cV")
            psQ_free = t_qb
            xn_rd = t_qb
            t_rope = rope_emit(qstg, qtb, csq, snq, j,
                               [t_qb, t_csq, t_snq], rope_t)
            wR = [t_rope, t_id] + ([psR_free] if psR_free else [])
            t_tr = None
            for h in range(NH):
                t_tr = P.emit("tensor", lambda e, h=h: e.transpose(
                    out=psB[2][:, h * 128:(h + 1) * 128],
                    in_=qtb[:, h * 128:(h + 1) * 128], identity=identsb[:]),
                    waits=wR if h == 0 else (), inc="cT" if h == NH - 1 else None)
            t_qe = None
            for h in range(NH):
                t_qe = P.emit("scalar", lambda e, h=h, j=j: e.copy(
                    out=qf[:, h * TOK + j * 128:h * TOK + (j + 1) * 128],
                    in_=psB[2][:, h * 128:(h + 1) * 128]),
                    waits=[t_tr] if h == 0 else (),
                    inc="cC" if h == NH - 1 else None)
            psR_free = t_qe

    if "attn" in phases:
        P.barrier()
        s_sb = [sb(f"s_sb{i}", [128, 4096], F32, B0 + i * 16 * KB)
                for i in range(2)]
        p_bf = [sb(f"p_bf{i}", [128, 4096], BF16, B0 + 32 * KB + i * 8 * KB)
                for i in range(2)]
        pt_sb = [sb(f"pt_sb{i}", [128, 1024], BF16, B0 + 48 * KB + i * 2 * KB)
                 for i in range(2)]
        o_bf2 = [sb("o_bf", [128, CKV], BF16, B0 + 52 * KB),
                 sb("o_bf1", [128, CKV], BF16, B0 + 109 * KB)]
        wo_sb = sb("wo_sb", [128, NH * C], BF16, B0 + 53 * KB)
        bo_rep = sb("bo_rep", [128, C], F32, B0 + 69 * KB)
        xo_buf = [sb(f"xo{i}", [128, C], F32, B0 + 77 * KB + i * 8 * KB)
                  for i in range(2)]
        ax_stage = [sb(f"axs{i}", [128, C], F32, B0 + 93 * KB + i * 8 * KB)
                    for i in range(2)]
        wo32 = [sb(f"wo32_{i}", [128, C], F32, B0 + i * 16 * KB)
                for i in range(2)]

        t_bo = dma("scalar", bo_rep[:], bo_d[0:1, :].partition_broadcast(128))
        wo_done = [None, None]
        for h in range(NH):
            buf = wo32[h % 2]
            w = [wo_done[h % 2]] if wo_done[h % 2] else []
            tl = dma("sync", buf[:], wo_d[h * 128:(h + 1) * 128, :], waits=w)
            wo_done[h % 2] = P.emit("vector", lambda e, h=h, b=buf: e.tensor_copy(
                out=wo_sb[:, h * C:(h + 1) * C], in_=b[:]), waits=[tl], inc="cV")
        t_wo = wo_done[1]

        bank_tok = {}
        stB = {"s_free": [t_wo, t_wo], "p_free": [None, None],
               "pt_free": [None, None], "o_free": [None, None],
               "xo_free": [None, None], "ax_free": [None, None],
               "exp_tok": [None, None], "sm": {}, "t_xo": {}, "ocast": {}}

        NPAIR = NBO * NH
        _jorder = []
        for k in range(NBO // 2):
            _jorder += [k, NBO - 1 - k]
        PAIRS = [(j, h) for a in range(NBO // 2)
                 for j, h in [( _jorder[2 * a], hh) for hh in range(NH)]
                 ] if False else [
            (j, h) for a in range(NBO // 2)
            for h in range(NH) for j in (_jorder[2 * a], _jorder[2 * a + 1])]

        def emit_scores_softmax(idx):
            j, h = PAIRS[idx]
            par = idx % 2
            nkb = 2 * j + 2
            nk = nkb * 128
            ssb = s_sb[par]
            pbf = p_bf[par]
            ncol = 0 if par == 0 else 8
            icol = 1 if par == 0 else 9
            acol = 2 if par == 0 else 3

            chunks = []
            kb0 = 0
            while kb0 < nkb:
                n = min(4, nkb - kb0)
                chunks.append((kb0, n))
                kb0 += n
            evs = []
            for ci, (kb0, n) in enumerate(chunks):
                ps = psF[ci % 2]
                w = [bank_tok.get(id(ps))] if bank_tok.get(id(ps)) else []
                if ci == 0 and stB["s_free"][par]:
                    w.append(stB["s_free"][par])
                t_s = P.emit("tensor", lambda e, ps=ps, h=h, j=j, kb0=kb0, n=n:
                             e.matmul(
                    out=ps[:, 0:n * 128],
                    lhsT=qf[:, h * TOK + j * 128:h * TOK + (j + 1) * 128],
                    rhs=kf[:, h * T + kb0 * 128:h * T + (kb0 + n) * 128],
                    start=True, stop=True), waits=w, inc="cT")
                if ci % 2 == 0:
                    ev = P.emit("scalar", lambda e, ps=ps, kb0=kb0, n=n, ssb=ssb:
                                e.copy(
                        out=ssb[:, kb0 * 128:(kb0 + n) * 128],
                        in_=ps[:, 0:n * 128]), waits=[t_s], inc="cC")
                else:
                    ev = P.emit("vector", lambda e, ps=ps, kb0=kb0, n=n, ssb=ssb:
                                e.tensor_copy(
                        out=ssb[:, kb0 * 128:(kb0 + n) * 128],
                        in_=ps[:, 0:n * 128]), waits=[t_s], inc="cV")
                if ci == len(chunks) - 1:
                    ev = P.emit("vector", lambda e, nk=nk, ssb=ssb: e.tensor_add(
                        out=ssb[:, nk - 256:nk],
                        in0=ssb[:, nk - 256:nk], in1=masksb[:]),
                        waits=[ev, t_mask], inc="cV")
                bank_tok[id(ps)] = ev
                evs.append(ev)
            t_max = P.emit("vector", lambda e, nk=nk, ssb=ssb, nc_=ncol:
                           e.reduce_max(
                out=ccol[:, nc_:nc_ + 1], in_=ssb[:, 0:nk], axis=AXX,
                negate=True),
                waits=evs + [stB["exp_tok"][par]], self_wait=True, inc="cV")
            t_exp = P.emit("scalar", lambda e, nk=nk, ssb=ssb, nc_=ncol, ac=acol:
                           e.activation(
                out=ssb[:, 0:nk], in_=ssb[:, 0:nk], func=AF.Exp,
                bias=ccol[:, nc_:nc_ + 1], scale=1.0,
                accum_out=ccolS[:, ac:ac + 1]), waits=[t_max], inc="cC")
            stB["exp_tok"][par] = t_exp
            t_inv = P.emit("vector", lambda e, ic=icol, ac=acol: e.reciprocal(
                out=ccol[:, ic:ic + 1], in_=ccolS[:, ac:ac + 1]),
                waits=[t_exp], inc="cV")
            w = [t_inv] + ([stB["p_free"][par]] if stB["p_free"][par] else [])
            t_pcast = P.emit("vector", lambda e, nk=nk, ssb=ssb, pbf=pbf,
                             ic=icol: e.tensor_single_scalar(
                out=pbf[:, 0:nk], in_=ssb[:, 0:nk], scalar=ccol[:, ic:ic + 1],
                op=ALU.mult), waits=w, self_wait=True, inc="cV")
            stB["s_free"][par] = t_pcast
            stB["sm"][idx] = t_pcast

        def emit_pt_pv(idx):
            j, h = PAIRS[idx]
            par = idx % 2
            nkb = 2 * j + 2
            pbf = p_bf[par]
            t_pcast = stB["sm"].pop(idx)
            psO = psF[2 + par]
            t_pv = None
            nbat = (nkb + 7) // 8
            ev_of = {}

            def emit_trans(bi):
                pbk = psB[2 + bi % 2]
                pts = pt_sb[bi % 2]
                k0, k1 = bi * 8, min(bi * 8 + 8, nkb)
                wb = [t_pcast, t_id] + (
                    [bank_tok.get(id(pbk))] if bank_tok.get(id(pbk)) else [])
                t_tr = None
                for k in range(k0, k1):
                    t_tr = P.emit("tensor", lambda e, pbk=pbk, k=k, k0=k0,
                                  pbf=pbf: e.transpose(
                        out=pbk[:, (k - k0) * 128:(k - k0 + 1) * 128],
                        in_=pbf[:, k * 128:(k + 1) * 128],
                        identity=identsb[:]),
                        waits=wb if k == k0 else (),
                        inc="cT" if k == k1 - 1 else None)
                t_ev = P.emit("scalar", lambda e, pbk=pbk, k0=k0, k1=k1,
                              pts=pts: e.copy(
                    out=pts[:, 0:(k1 - k0) * 128],
                    in_=pbk[:, 0:(k1 - k0) * 128]),
                    waits=[t_tr] + (
                        [stB["pt_free"][bi % 2]] if stB["pt_free"][bi % 2] else []),
                    inc="cC")
                bank_tok[id(pbk)] = t_ev
                ev_of[bi] = t_ev

            def emit_pv(bi):
                nonlocal t_pv
                pts = pt_sb[bi % 2]
                k0, k1 = bi * 8, min(bi * 8 + 8, nkb)
                wO = [ev_of.pop(bi)]
                if bi == 0 and bank_tok.get(id(psO)):
                    wO.append(bank_tok.get(id(psO)))
                for k in range(k0, k1):
                    t_pv = P.emit("tensor", lambda e, psO=psO, k=k, k0=k0, h=h,
                                  pts=pts, first=(k == 0),
                                  last=(k == nkb - 1): e.matmul(
                        out=psO[:, 0:128],
                        lhsT=vt[:, k * CKV + h * 128:k * CKV + (h + 1) * 128],
                        rhs=pts[:, (k - k0) * 128:(k - k0 + 1) * 128],
                        start=first, stop=last),
                        waits=wO if k == k0 else (),
                        inc="cT" if k == nkb - 1 else None)
                stB["pt_free"][bi % 2] = t_pv

            for bi in range(nbat):
                emit_trans(bi)
                if bi >= 1:
                    emit_pv(bi - 1)
            emit_pv(nbat - 1)
            stB["p_free"][par] = t_pv
            js = 0 if j < NBO // 2 else 1
            obf = o_bf2[js]
            wOe = [t_pv] + ([stB["o_free"][js]]
                            if h == 0 and stB["o_free"][js] else [])
            t_ocast = P.emit("scalar", lambda e, psO=psO, h=h, obf=obf: e.copy(
                out=obf[:, h * 128:(h + 1) * 128], in_=psO[:, 0:128]),
                waits=wOe, inc="cC")
            bank_tok[id(psO)] = t_ocast
            stB["ocast"][j] = t_ocast

        def emit_wo(j):
            axs = ax_stage[j % 2]
            xo = xo_buf[j % 2]
            js = 0 if j < NBO // 2 else 1
            obf = o_bf2[js]
            t_ocast = stB["ocast"].pop(j)
            t_xo = dma("scalar", xo[:], x_own[j * 128:(j + 1) * 128, :],
                       waits=[stB["xo_free"][j % 2]]
                       if stB["xo_free"][j % 2] else [])
            t_ao = None
            t_ax = None
            for oc in range(4):
                psA = psF[oc % 2]
                wA = [t_ocast, t_wo] +                     ([bank_tok.get(id(psA))] if bank_tok.get(id(psA)) else [])
                for h in range(NH):
                    t_ao = P.emit("tensor", lambda e, psA=psA, h=h, oc=oc,
                                  obf=obf: e.matmul(
                        out=psA[:], lhsT=obf[:, h * 128:(h + 1) * 128],
                        rhs=wo_sb[:, h * C + oc * 512:h * C + (oc + 1) * 512],
                        start=(h == 0), stop=(h == NH - 1)),
                        waits=wA if h == 0 else (),
                        inc="cT" if h == NH - 1 else None)
                wE = [t_ao, t_bo, t_xo]
                if stB["ax_free"][j % 2] and oc == 0:
                    wE.append(stB["ax_free"][j % 2])
                t_axp = P.emit("vector", lambda e, psA=psA, oc=oc, axs=axs:
                               e.tensor_add(
                    out=axs[:, oc * 512:(oc + 1) * 512], in0=psA[:],
                    in1=bo_rep[:, oc * 512:(oc + 1) * 512]), waits=wE, inc="cV")
                t_ax = P.emit("vector", lambda e, oc=oc, axs=axs, xo=xo:
                              e.tensor_add(
                    out=axs[:, oc * 512:(oc + 1) * 512],
                    in0=axs[:, oc * 512:(oc + 1) * 512],
                    in1=xo[:, oc * 512:(oc + 1) * 512]), inc="cV")
                bank_tok[id(psA)] = t_axp
            stB["xo_free"][j % 2] = t_ax
            stB["o_free"][js] = t_ao
            t_axd = dma("gpsimd", ax_s[j * 128:(j + 1) * 128, :], axs[:],
                        waits=[t_ax])
            stB["ax_free"][j % 2] = t_axd

        done_heads = {}
        for idx in range(NPAIR + 1):
            if idx < NPAIR:
                emit_scores_softmax(idx)
            if idx >= 1:
                jprev, hprev = PAIRS[idx - 1]
                emit_pt_pv(idx - 1)
                done_heads[jprev] = done_heads.get(jprev, 0) + 1
                if done_heads[jprev] == NH:
                    emit_wo(jprev)

    if "gate" in phases:
        P.barrier()
        qx_f = sb("qx_f", [128, 16 * TOK], BF16, 0)
        w1_32 = [sb(f"w1_32_{i}", [128, C], F32, 64 * KB + i * 8 * KB)
                 for i in range(2)]
        w1u = [sb(f"w1u{i}", [128, C], F32, B0 + i * 8 * KB) for i in range(2)]
        w1q = [sb(f"w1q{i}", [128, C], BF16, B0 + 16 * KB + i * 4 * KB)
               for i in range(2)]
        w1qT_g = [sb(f"w1qT_g{i}", [128, 16 * 512], BF16,
                     B0 + 24 * KB + i * 16 * KB) for i in range(2)]
        w1qT_u = [sb(f"w1qT_u{i}", [128, 16 * 512], BF16,
                     B0 + 56 * KB + i * 16 * KB) for i in range(2)]
        silu_st = [sb(f"silu{i}", [128, 512], F32, 88 * KB + i * 2 * KB)
                   for i in range(2)]
        t_st = [sb(f"tst{i}", [128, 512], BF16, 92 * KB + i * KB)
                for i in range(2)]

        qx_r = qx_s.rearrange("p (j c w) -> p j c w", j=NBO, c=16, w=128)
        qxl_toks = []
        for c in range(16):
            qxl_toks.append(dma(("sync", "scalar", "gpsimd")[c % 3],
                                qx_f[:, c * TOK:(c + 1) * TOK],
                                qx_r[:, :, c, :]))
        qxl_last = qxl_toks[13:16]

        stc = {"w1_free": [None, None], "wu_free": [None, None],
               "wq_free": [None, None], "bank": {},
               "gT_rd": [None, None], "uT_rd": [None, None],
               "psG": [None, None], "psU": [None, None],
               "silu_free": [None, None], "tst_free": [None, None],
               "rmx_tok": None, "act_col_rd": [None, None]}

        def quant_chunk_chain(row0, i):
            par = i % 2
            cbase = 2 if par == 0 else 8
            buf = w1_32[par]
            w = [stc["w1_free"][par]] if stc["w1_free"][par] else []
            tl = dma("sync", buf[:],
                     w1_d[row0 + i * 128:row0 + (i + 1) * 128, :], waits=w)
            t1 = P.emit("vector", lambda e, b=buf, cb=cbase: e.reduce_max(
                out=ccol[:, cb:cb + 1], in_=b[:], axis=AXX,
                apply_absolute_value=True),
                waits=[tl] + ([stc["act_col_rd"][par]]
                              if stc["act_col_rd"][par] else []), inc="cV")
            t1 = P.emit("vector", lambda e, cb=cbase: e.tensor_single_scalar(
                out=ccol[:, cb:cb + 1], in_=ccol[:, cb:cb + 1],
                scalar=cstb[:, 5:6], op=ALU.mult),
                waits=[t1], self_wait=True, inc="cV")
            t1 = P.emit("vector", lambda e, cb=cbase: e.tensor_single_scalar(
                out=ccol[:, cb + 1:cb + 2], in_=ccol[:, cb:cb + 1],
                scalar=cstb[:, 6:7], op=ALU.max),
                waits=[t1], self_wait=True, inc="cV")
            t1 = P.emit("vector", lambda e, cb=cbase: e.reciprocal(
                out=ccol[:, cb + 2:cb + 3], in_=ccol[:, cb + 1:cb + 2]),
                waits=[t1], self_wait=True, inc="cV")
            wu = w1u[par]
            wq = w1q[par]
            tu = P.emit("scalar", lambda e, b=buf, cb=cbase, wu=wu:
                        e.activation(
                out=wu[:], in_=b[:], func=AF.Identity, bias=meff[:, 0:1],
                scale=ccol[:, cb + 2:cb + 3]),
                waits=[t1] + ([stc["wu_free"][par]]
                              if stc["wu_free"][par] else []), inc="cC")
            stc["w1_free"][par] = tu
            stc["act_col_rd"][par] = tu
            P.emit("vector", lambda e, wu=wu: e.tensor_single_scalar(
                out=wu[:], in_=wu[:], scalar=meff[:, 0:1],
                op=ALU.subtract), waits=[tu])
            tq = P.emit("vector", lambda e, cb=cbase, wu=wu, wq=wq:
                        e.tensor_single_scalar(
                out=wq[:], in_=wu[:], scalar=ccol[:, cb + 1:cb + 2],
                op=ALU.mult),
                waits=[stc["wq_free"][par]] if stc["wq_free"][par] else [],
                inc="cV")
            stc["wu_free"][par] = tq
            return tq

        def quant_chunk_transposes(dst_T, i, tq, t_prev_rd):
            evs = transpose_group(w1q[i % 2], list(range(16)), dst_T,
                                  [cc * 512 + i * 128 for cc in range(16)],
                                  [psB[0], psB[1]], stc["bank"],
                                  [tq, t_id, t_prev_rd], evac_eng="scalar")
            stc["wq_free"][i % 2] = evs[-1]
            return evs[-1]

        def emit_chunk(pc2, k, phase):
            pp2 = pc2 % 2
            mat, i = (0, k) if k < 4 else (1, k - 4)
            row0 = 512 * pc2 + (I if mat else 0)
            if phase == 0:
                stc["ch_tq"] = quant_chunk_chain(row0, i)
            else:
                dst = (w1qT_g if mat == 0 else w1qT_u)[pp2]
                rd = (stc["gT_rd"] if mat == 0 else stc["uT_rd"])[pp2]
                tok = quant_chunk_transposes(dst, i, stc["ch_tq"], rd)
                (stc["gT_wr"] if mat == 0 else stc["uT_wr"])[pp2] = tok

        stc["gT_wr"] = [None, None]
        stc["uT_wr"] = [None, None]
        stc["ch_tq"] = None
        for k in range(8):
            emit_chunk(0, k, 0)
            emit_chunk(0, k, 1)
        for pc in range(11):
            pp = pc % 2
            tg = stc["gT_wr"][pp]
            tu_ = stc["uT_wr"][pp]
            t_gmm = None
            t_umm = None
            for blk in range(NBO):
                if pc < 10:
                    emit_chunk(pc + 1, blk // 2, blk % 2)
                psg = psF[0 + 2 * (blk % 2)]
                psu = psF[1 + 2 * (blk % 2)]
                wg = [tg, tu_] + qxl_last + (
                    [stc["psG"][blk % 2]] if stc["psG"][blk % 2] else [])
                for c in range(16):
                    t_gmm = P.emit("tensor", lambda e, c=c, blk=blk, psg=psg,
                                   wT=w1qT_g[pp]: e.matmul(
                        out=psg[:], lhsT=qx_f[:, c * TOK + blk * 128:
                                             c * TOK + (blk + 1) * 128],
                        rhs=wT[:, c * 512:(c + 1) * 512],
                        start=(c == 0), stop=(c == 15)),
                        waits=wg if c == 0 else (),
                        inc="cT" if c == 15 else None)
                wu2 = [stc["psU"][blk % 2]] if stc["psU"][blk % 2] else []
                for c in range(16):
                    t_umm = P.emit("tensor", lambda e, c=c, blk=blk, psu=psu,
                                   wT=w1qT_u[pp]: e.matmul(
                        out=psu[:], lhsT=qx_f[:, c * TOK + blk * 128:
                                             c * TOK + (blk + 1) * 128],
                        rhs=wT[:, c * 512:(c + 1) * 512],
                        start=(c == 0), stop=(c == 15)),
                        waits=wu2 if c == 0 else (),
                        inc="cT" if c == 15 else None)
                sil = silu_st[blk % 2]
                tst = t_st[blk % 2]
                t_sil = P.emit("scalar", lambda e, psg=psg, blk=blk, sil=sil:
                               e.activation(
                    out=sil[:], in_=psg[:], func=AF.Silu,
                    bias=0.0, scale=sxc[:, blk:blk + 1]),
                    waits=[t_gmm] + ([stc["silu_free"][blk % 2]]
                                     if stc["silu_free"][blk % 2] else []),
                    inc="cC")
                stc["psG"][blk % 2] = t_sil
                t_t = P.emit("vector", lambda e, psu=psu, sil=sil, tst=tst:
                             e.tensor_mul(out=tst[:], in0=sil[:], in1=psu[:]),
                    waits=[t_umm, t_sil] + ([stc["tst_free"][blk % 2]]
                                            if stc["tst_free"][blk % 2] else []),
                    inc="cV")
                stc["psU"][blk % 2] = t_t
                stc["silu_free"][blk % 2] = t_t
                t_am = P.emit("vector", lambda e, tst=tst: e.reduce_max(
                    out=ccol[:, 6:7], in_=tst[:], axis=AXX,
                    apply_absolute_value=True), inc="cV")
                if pc == 0:
                    t_rm = P.emit("vector", lambda e, blk=blk: e.tensor_copy(
                        out=rmx[:, blk:blk + 1], in_=ccol[:, 6:7]),
                        waits=[t_am], self_wait=True, inc="cV")
                else:
                    t_rm = P.emit("vector", lambda e, blk=blk: e.tensor_max(
                        out=rmx[:, blk:blk + 1], in0=rmx[:, blk:blk + 1],
                        in1=ccol[:, 6:7]),
                        waits=[t_am, stc["rmx_tok"]], self_wait=True, inc="cV")
                stc["rmx_tok"] = t_rm
                stc["tst_free"][blk % 2] = dma(
                    "gpsimd", t_sd[blk * 128:(blk + 1) * 128,
                                   pc * 512:(pc + 1) * 512], tst[:],
                    waits=[t_t])
            stc["gT_rd"][pp] = t_gmm
            stc["uT_rd"][pp] = t_umm

    if "down" in phases:
        P.barrier()
        t_rd = [sb(f"t_rd{i}", [128, I], BF16, i * 11 * KB) for i in range(2)]
        hqT_st = [sb(f"hqT_st{i}", [128, I], BF16, B0 + i * 12 * KB)
                  for i in range(2)]
        tu_st = [sb(f"tu_st{i}", [128, I], F32, B0 + 24 * KB + i * 22 * KB)
                 for i in range(2)]
        hq_st = [sb(f"hq_st{i}", [128, I], BF16, B0 + 68 * KB + i * 12 * KB)
                 for i in range(2)]
        std = {"t_free": [None, None], "tu_free": [None, None],
               "hq_free": [None, None], "bank": {}, "hqT_free": [None, None]}
        t1 = P.emit("vector", lambda e: e.tensor_mul(
            out=shcol[:], in0=rmx[:], in1=sxc[:]), inc="cV")
        t1 = P.emit("vector", lambda e: e.tensor_single_scalar(
            out=shcol[:], in_=shcol[:], scalar=cstb[:, 5:6], op=ALU.mult),
            waits=[t1], self_wait=True, inc="cV")
        t1 = P.emit("vector", lambda e: e.tensor_single_scalar(
            out=shcol[:], in_=shcol[:], scalar=cstb[:, 6:7], op=ALU.max),
            waits=[t1], self_wait=True, inc="cV")
        t1 = P.emit("vector", lambda e: e.reciprocal(
            out=rscol[:], in_=shcol[:]), waits=[t1], self_wait=True, inc="cV")
        t_rs = P.emit("vector", lambda e: e.tensor_mul(
            out=rscol[:], in0=rscol[:], in1=sxc[:]), waits=[t1],
            self_wait=True, inc="cV")
        for blk in range(NBO):
            par = blk % 2
            buf = t_rd[par]
            w = [std["t_free"][par]] if std["t_free"][par] else []
            tl = dma("sync", buf[:], t_sd[blk * 128:(blk + 1) * 128, :], waits=w)
            tus = tu_st[par]
            ttu = P.emit("scalar", lambda e, b=buf, blk=blk, tus=tus:
                         e.activation(
                out=tus[:], in_=b[:], func=AF.Identity, bias=meff[:, 0:1],
                scale=rscol[:, blk:blk + 1]),
                waits=[t_rs, tl] +
                ([std["tu_free"][par]] if std["tu_free"][par] else []),
                inc="cC")
            std["t_free"][par] = ttu
            hqs = hq_st[par]
            thq = P.emit("vector", lambda e, tus=tus, hqs=hqs:
                         e.tensor_single_scalar(
                out=hqs[:], in_=tus[:], scalar=meff[:, 0:1], op=ALU.subtract),
                waits=[ttu] + ([std["hq_free"][par]] if std["hq_free"][par] else []),
                inc="cV")
            std["tu_free"][par] = thq
            hqT = hqT_st[par]
            evs = transpose_group(hqs, list(range(44)), hqT,
                                  [k * 128 for k in range(44)],
                                  [psB[0], psB[1]], std["bank"],
                                  [thq, t_id] +
                                  ([std["hqT_free"][par]] if std["hqT_free"][par] else []),
                                  evac_eng="vector")
            std["hq_free"][par] = evs[-1]
            std["hqT_free"][par] = dma(
                "gpsimd", hq_s[:, blk * I:(blk + 1) * I], hqT[:], waits=evs)
        P.barrier()
        hqb = [sb(f"hqb{i}", [128, I], BF16, i * 11 * KB) for i in range(2)]
        w2_32 = sb("w2_32", [128, I], F32, 22 * KB)
        w2u = sb("w2u", [128, I], F32, 44 * KB)
        w2q = sb("w2q", [128, I], BF16, 66 * KB)
        w2qT = sb("w2qT", [128, 44 * 1024], BF16, B0)
        axb = [sb(f"axb{i}", [128, 1024], F32, B0 + 88 * KB + i * 4 * KB)
               for i in range(2)]
        ot_st = [sb(f"ot{i}", [128, 512], F32, B0 + 96 * KB + i * 2 * KB)
                 for i in range(2)]
        std2 = {"w2_free": None, "wu_free": None, "wq_free": None, "bank": {},
                "hqb_free": [None, None], "axb_free": [None, None],
                "ot_free": [None, None], "psD": [None, None], "w2T_rd": None}
        for sup in range(2):
            ev = None
            for hc in range(8):
                row0 = sup * 1024 + hc * 128
                tl = dma("sync", w2_32[:], w2_d[row0:row0 + 128, :],
                         waits=[std2["w2_free"]] if std2["w2_free"] else [])
                t1 = P.emit("vector", lambda e: e.reduce_max(
                    out=ccol[:, 2:3], in_=w2_32[:], axis=AXX,
                    apply_absolute_value=True), waits=[tl], inc="cV")
                t1 = P.emit("vector", lambda e: e.tensor_single_scalar(
                    out=ccol[:, 2:3], in_=ccol[:, 2:3], scalar=cstb[:, 5:6],
                    op=ALU.mult), waits=[t1], self_wait=True, inc="cV")
                t1 = P.emit("vector", lambda e: e.tensor_single_scalar(
                    out=ccol[:, 3:4], in_=ccol[:, 2:3], scalar=cstb[:, 6:7],
                    op=ALU.max), waits=[t1], self_wait=True, inc="cV")
                t1 = P.emit("vector", lambda e: e.reciprocal(
                    out=ccol[:, 4:5], in_=ccol[:, 3:4]),
                    waits=[t1, std2.get("act_col_rd")], self_wait=True, inc="cV")
                tu2 = P.emit("scalar", lambda e: e.activation(
                    out=w2u[:], in_=w2_32[:], func=AF.Identity,
                    bias=meff[:, 0:1], scale=ccol[:, 4:5]),
                    waits=[t1] + ([std2["wu_free"]] if std2["wu_free"] else []),
                    inc="cC")
                std2["w2_free"] = tu2
                std2["act_col_rd"] = tu2
                P.emit("vector", lambda e: e.tensor_single_scalar(
                    out=w2u[:], in_=w2u[:], scalar=meff[:, 0:1],
                    op=ALU.subtract), waits=[tu2])
                tq2 = P.emit("vector", lambda e: e.tensor_single_scalar(
                    out=w2q[:], in_=w2u[:], scalar=ccol[:, 3:4], op=ALU.mult),
                    waits=[std2["wq_free"]] if std2["wq_free"] else [],
                    inc="cV")
                std2["wu_free"] = tq2
                evs = transpose_group(w2q, list(range(44)), w2qT,
                                      [cc * 1024 + hc * 128 for cc in range(44)],
                                      [psB[0], psB[1]], std2["bank"],
                                      [tq2, t_id] +
                                      ([std2["w2T_rd"]] if std2["w2T_rd"] else []),
                                      evac_eng="scalar")
                std2["wq_free"] = evs[-1]
                ev = evs[-1]
            t_dmm = None
            for blk in range(NBO):
                hb = hqb[blk % 2]
                w = [std2["hqb_free"][blk % 2]] if std2["hqb_free"][blk % 2] else []
                tlh = dma("sync", hb[:], hq_s[:, blk * I:(blk + 1) * I],
                          waits=w)
                ab = axb[blk % 2]
                tax = dma("scalar", ab[:],
                          ax_s[blk * 128:(blk + 1) * 128,
                               sup * 1024:(sup + 1) * 1024],
                          waits=[std2["axb_free"][blk % 2]]
                          if std2["axb_free"][blk % 2] else [])
                for hc2 in range(2):
                    psd = psF[hc2]
                    wD = [ev, tlh] +                         ([std2["psD"][hc2]] if std2["psD"][hc2] else [])
                    for c in range(44):
                        t_dmm = P.emit("tensor", lambda e, c=c, hc2=hc2, hb=hb,
                                       psd=psd: e.matmul(
                            out=psd[:], lhsT=hb[:, c * 128:(c + 1) * 128],
                            rhs=w2qT[:, c * 1024 + hc2 * 512:
                                     c * 1024 + (hc2 + 1) * 512],
                            start=(c == 0), stop=(c == 43)),
                            waits=wD if c == 0 else (),
                            inc="cT" if c == 43 else None)
                    ot = ot_st[hc2]
                    t_o1 = P.emit("vector", lambda e, psd=psd, blk=blk, ot=ot:
                                  e.tensor_single_scalar(
                        out=ot[:], in_=psd[:], scalar=shcol[:, blk:blk + 1],
                        op=ALU.mult),
                        waits=[t_dmm, tax] +
                        ([std2["ot_free"][hc2]] if std2["ot_free"][hc2] else []),
                        inc="cV")
                    t_o2 = P.emit("vector", lambda e, ot=ot, ab=ab, hc2=hc2:
                                  e.tensor_add(
                        out=ot[:], in0=ot[:],
                        in1=ab[:, hc2 * 512:(hc2 + 1) * 512]), inc="cV")
                    std2["psD"][hc2] = t_o1
                    tod = dma("gpsimd",
                              out_d[blk * 128:(blk + 1) * 128,
                                    sup * 1024 + hc2 * 512:
                                    sup * 1024 + (hc2 + 1) * 512], ot[:],
                              waits=[t_o2])
                    std2["ot_free"][hc2] = tod
                std2["hqb_free"][blk % 2] = t_dmm
                std2["axb_free"][blk % 2] = t_o2
            std2["w2T_rd"] = t_dmm

    if dbg:
        P.barrier()
        if "sh" in dbg:
            P.emit("vector", lambda e: e.tensor_copy(out=shcol[:, 0:1], in_=meff[:]))
            P.emit("vector", lambda e: e.tensor_copy(out=shcol[:, 1:2], in_=scol[:]))
            P.emit("vector", lambda e: e.tensor_copy(out=shcol[:, 2:3], in_=cstb[:, 0:1]))
            P.emit("vector", lambda e: e.tensor_copy(out=shcol[:, 3:4], in_=cstb[:, 1:2]), inc="cV")
            P.barrier()
        last = None
        for name, ap in dbg.items():
            src = {"kf": kf, "vt": vt, "qf": qf, "sx": sxc, "sh": shcol}.get(name)
            if src is not None:
                last = dma("gpsimd", ap, src[:], waits=[P.now("dG")])
            elif name in ("xn_s", "qx_s"):
                s = xn_s if name == "xn_s" else qx_s
                last = dma("gpsimd", ap, s, waits=[P.now("dG")])
            elif name == "ax_s":
                for jj in range(NBO):
                    last = dma("gpsimd", ap[jj * 128:(jj + 1) * 128, :],
                               ax_s[jj * 128:(jj + 1) * 128, :],
                               waits=[P.now("dG")])
            elif name == "t_s":
                for jj in range(NBO):
                    last = dma("gpsimd", ap[jj * 128:(jj + 1) * 128, :],
                               t_sd[jj * 128:(jj + 1) * 128, :],
                               waits=[P.now("dG")])
            elif name == "hq_s":
                last = dma("gpsimd", ap, hq_s, waits=[P.now("dG")])
            elif name == "stt":
                last = dma("gpsimd", ap, stall[:], waits=[P.now("dG")])

    if "down" not in phases:
        P.barrier()
        z = sb("zout", [128, C], F32, B0 + 88 * KB)
        tz = P.emit("vector", lambda e: e.memset(z[:], 0.0), inc="cV")
        for j in range(NBO):
            dma("gpsimd", out_d[j * 128:(j + 1) * 128, :], z[:], waits=[tz])

    P.barrier()
    P.wait_only("sync", [P.now("dS"), P.now("dC"), P.now("dG")])

    with nc.Block() as block:
        for eng_name, deco in (("sync", block.sync), ("vector", block.vector),
                               ("scalar", block.scalar),
                               ("tensor", block.tensor),
                               ("gpsimd", block.gpsimd)):
            def body(e, eng_name=eng_name):
                for fn in P.streams[eng_name]:
                    fn(e)
            deco(body)
    es.close()
    return nc



def _weights_dict(Wq, bq, Wk, bk, Wv, bv, Wo, bo, w1, w2):
    return {
        "wq": np.ascontiguousarray(np.asarray(Wq, np.float32)),
        "wk": np.ascontiguousarray(np.asarray(Wk, np.float32)),
        "wv": np.ascontiguousarray(np.asarray(Wv, np.float32)),
        "wo": np.ascontiguousarray(np.asarray(Wo, np.float32)),
        "bq": np.asarray(bq, np.float32).reshape(1, C),
        "bk": np.asarray(bk, np.float32).reshape(1, CKV),
        "bv": np.asarray(bv, np.float32).reshape(1, CKV),
        "bo": np.asarray(bo, np.float32).reshape(1, C),
        "w1": np.ascontiguousarray(np.asarray(w1, np.float32)),
        "w2": np.ascontiguousarray(np.asarray(w2, np.float32)),
    }


def kernel(L0_attn_x, L0_attn_Wq, L0_attn_bq, L0_attn_Wk, L0_attn_bk,
           L0_attn_Wv, L0_attn_bv, L0_attn_Wo, L0_attn_bo,
           L0_ffn_w_gate_up, L0_ffn_w_down):
    x = np.ascontiguousarray(np.asarray(L0_attn_x, dtype=np.float32))
    weights = _weights_dict(L0_attn_Wq, L0_attn_bq, L0_attn_Wk, L0_attn_bk,
                            L0_attn_Wv, L0_attn_bv, L0_attn_Wo, L0_attn_bo,
                            L0_ffn_w_gate_up, L0_ffn_w_down)
    nc = build_nc()
    in_maps = [_host_inputs_for_core(core, x, weights) for core in range(8)]
    res = run_bass_kernel_spmd(nc, in_maps, core_ids=list(range(8)))
    out = np.empty((B, T, C), np.float32)
    for core in range(8):
        b, p = core // 2, core % 2
        shard = np.asarray(res.results[core]["out"]).reshape(NBO, 128, C)
        out[b].reshape(NBA, 128, C)[p::2] = shard
    return out
